# version 04 (final)
# baseline (speedup 1.0000x reference)
import contextlib
import numpy as np
import concourse.bass as bass
import concourse.mybir as mybir
from concourse.bass_utils import run_bass_kernel_spmd

F32 = mybir.dt.float32
BF16 = mybir.dt.bfloat16
FP8 = mybir.dt.float8e4
U8 = mybir.dt.uint8
ACTF = mybir.ActivationFunctionType
ALU = mybir.AluOpType
AX = mybir.AxisListType

DEBUG_P1 = False
HWDBG = False
PHASE_CAP = "full"
NBLK_CAP = None
SKIP_Q8 = False
SKIP_SPILL = False
SKIP_TRAIL = False
P1_PART = None
KILL = set()
B, T, D = 2, 4096, 2048
HPC, KC, IC = 4, 16, 48
BLK = 256
NBLK = T // BLK
EPS_LN = 1e-5

CB_EXPS, CB_EXPB, CB_EPSLN, CB_RKB, CB_RQB, CB_RQS, CB_INVD, CB_RKS, CB_ONE = range(9)
NCB = 9


class Sem:
    def __init__(self, stack, nc, name):
        self.h = stack.enter_context(nc.semaphore(name))
        self.n = 0

    def inc(self, instr, k=1):
        instr.then_inc(self.h, k)
        self.n += k
        return self.n

    def dma(self, instr):
        return self.inc(instr, 16)


def build_nc():
    nc = bass.Bass("TRN2", target_bir_lowering=False, detect_race_conditions=False)

    d_x = nc.dram_tensor("x_b", [T, D], F32, kind="ExternalInput").ap()
    d_xres = nc.dram_tensor("x_res", [1024, D], F32, kind="ExternalInput").ap()
    d_wq = nc.dram_tensor("wq_h", [512, D], F32, kind="ExternalInput").ap()
    d_wk = nc.dram_tensor("wk_h", [512, D], F32, kind="ExternalInput").ap()
    d_wv = nc.dram_tensor("wv_h", [512, D], F32, kind="ExternalInput").ap()
    d_wo = nc.dram_tensor("wo_h", [D, 512], F32, kind="ExternalInput").ap()
    d_ropec = nc.dram_tensor("ropec", [128, 2], F32, kind="ExternalInput").ap()
    d_ropes = nc.dram_tensor("ropes", [128, 2], F32, kind="ExternalInput").ap()
    d_qwp = nc.dram_tensor("qwp", [128, 2], F32, kind="ExternalInput").ap()
    d_kwp = nc.dram_tensor("kwp", [128, 2], F32, kind="ExternalInput").ap()
    d_gw = nc.dram_tensor("gate_w", [D, 6144], F32, kind="ExternalInput").ap()
    d_uw = nc.dram_tensor("up_w", [D, 6144], F32, kind="ExternalInput").ap()
    d_dw = nc.dram_tensor("down_w", [6144, D], F32, kind="ExternalInput").ap()
    d_gsc = nc.dram_tensor("gsct", [128, IC], F32, kind="ExternalInput").ap()
    d_usc = nc.dram_tensor("usct", [128, IC], F32, kind="ExternalInput").ap()
    d_dsc = nc.dram_tensor("dscr", [1, D], F32, kind="ExternalInput").ap()
    d_id32 = nc.dram_tensor("id32", [128, 128], F32, kind="ExternalInput").ap()
    d_idb = nc.dram_tensor("idb", [128, 128], BF16, kind="ExternalInput").ap()
    d_mask = nc.dram_tensor("mask", [128, 896], BF16, kind="ExternalInput").ap()
    d_cb = nc.dram_tensor("cb", [128, NCB], F32, kind="ExternalInput").ap()
    d_sel2 = nc.dram_tensor("sel2", [2, 128], F32, kind="ExternalInput").ap()
    d_ones1 = nc.dram_tensor("ones1b", [128, 1], BF16, kind="ExternalInput").ap()
    d_onesr = nc.dram_tensor("onesr", [1, 128], F32, kind="ExternalInput").ap()
    d_ones2 = nc.dram_tensor("ones2", [128, 128], F32, kind="ExternalInput").ap()

    d_y = nc.dram_tensor("y", [1024, D], F32, kind="ExternalOutput").ap()
    d_dbgb = nc.dram_tensor("dbgb", [8, 4096], BF16, kind="ExternalOutput").ap() if HWDBG else None
    d_dbgf = nc.dram_tensor("dbgf", [4, 2048], F32, kind="ExternalOutput").ap() if HWDBG else None

    d_ksp = nc.dram_tensor("k_sp", [HPC, 2, 128, T], BF16, kind="Internal").ap()
    d_qsp = nc.dram_tensor("q_sp", [HPC, 2, 128, T], BF16, kind="Internal").ap()
    d_vsp = nc.dram_tensor("v_sp", [HPC, 2, 128, T // 128, 128], BF16, kind="Internal").ap()
    d_wot = nc.dram_tensor("wot_sp", [HPC, 2, 128, D], BF16, kind="Internal").ap()
    d_g8 = nc.dram_tensor("g8t", [IC, 128, 16, 128], FP8, kind="Internal").ap()
    d_u8 = nc.dram_tensor("u8t", [IC, 128, 16, 128], FP8, kind="Internal").ap()
    d_d8 = nc.dram_tensor("d8t", [IC, 128, D], FP8, kind="Internal").ap()
    d_fsp = nc.dram_tensor("fuse_sp", [IC, 2, 128, 1024], BF16, kind="Internal").ap()
    d_x1 = nc.dram_tensor("x1_sp", [8, 128, D], F32, kind="Internal").ap()
    d_cin = [nc.dram_tensor(f"cin{q}", [1024, D], F32, kind="Internal").ap() for q in range(4)]
    d_dbgln = (nc.dram_tensor("dbg_ln", [32, 2, 128, D], BF16, kind="Internal").ap()
               if DEBUG_P1 else None)
    d_dbgfm = (nc.dram_tensor("dbg_fm", [NBLK, 2, 128, KC * BLK], BF16, kind="Internal").ap()
               if DEBUG_P1 else None)
    d_cout = [nc.dram_tensor(f"cout{q}", [256, D], F32, kind="Internal").ap() for q in range(4)]

    st = contextlib.ExitStack()
    st.__enter__()
    E = st.enter_context

    s_id32 = E(nc.sbuf_tensor("s_id32", [128, 128], F32))
    s_idb = E(nc.sbuf_tensor("s_idb", [128, 128], BF16))
    s_mask = E(nc.sbuf_tensor("s_mask", [128, 896], BF16))
    s_cb = E(nc.sbuf_tensor("s_cb", [128, NCB], F32))
    s_sel2 = E(nc.sbuf_tensor("s_sel2", [2, 128], F32))
    s_ones1 = E(nc.sbuf_tensor("s_ones1", [128, 1], BF16))
    s_onesr = E(nc.sbuf_tensor("s_onesr", [1, 128], F32))
    s_ones2 = E(nc.sbuf_tensor("s_ones2", [128, 128], F32))
    s_ropec = E(nc.sbuf_tensor("s_ropec", [128, 2], F32))
    s_ropes = E(nc.sbuf_tensor("s_ropes", [128, 2], F32))
    s_ropesn = E(nc.sbuf_tensor("s_ropesn", [128, 2], F32))
    s_qwp = E(nc.sbuf_tensor("s_qwp", [128, 2], F32))
    s_kwp = E(nc.sbuf_tensor("s_kwp", [128, 2], F32))
    s_gsc = E(nc.sbuf_tensor("s_gsc", [128, IC], F32))
    s_usc = E(nc.sbuf_tensor("s_usc", [128, IC], F32))
    s_rkc = E(nc.sbuf_tensor("s_rkc", [128, HPC * 32], F32))
    s_qin = [E(nc.sbuf_tensor(f"s_qin{i}", [128, 512], F32)) for i in range(2)]
    s_qo8 = [E(nc.sbuf_tensor(f"s_qo8{i}", [128, 512], FP8)) for i in range(2)]
    s_qo8b = [E(nc.sbuf_tensor(f"s_qo8b{i}", [128, 512], FP8)) for i in range(2)]

    names = ["pe", "act", "dve", "cc",
             "dcn", "dx0", "dx1", "dq8a", "dq8b",
             "dsp0", "dsp1", "dsp2", "dsp3", "dsp4", "dsp5", "dsp6", "dsp7",
             "dsv0", "dsv1", "dwot", "dal0", "dal1", "dy1", "dy1b",
             "dyl", "dylb", "dxr", "dxrb", "dgu0", "dgu1", "dfs0", "dfs1", "dx1s",
             "dfl", "dd8", "dx1l", "dyo", "ddsc"]
    SM = {}
    for nm in names:
        SM[nm] = Sem(st, nc, nm)
    S = type("SS", (), SM)

    blk = E(nc.Block())
    sync, ten, vec, sca, gps = blk.sync, blk.tensor, blk.vector, blk.scalar, blk.gpsimd

    def emit_dbgdump(wait_list):
        if not HWDBG:
            return

        def dbgdump(g_, wait_list=tuple(wait_list)):
            for hh, nn_ in wait_list:
                g_.wait_ge(hh, nn_)
            for i, src_ in [(0, d_ksp[0, 0, 0:1, :]), (1, d_ksp[0, 1, 0:1, :]),
                            (2, d_qsp[0, 0, 0:1, :]), (3, d_qsp[0, 1, 0:1, :]),
                            (4, d_vsp[0, 0, 0:1, :, :].rearrange("a j d -> a (j d)")),
                            (5, d_fsp[0, 0, 0:1, :])]:
                w = 1024 if i == 5 else 4096
                S.dsv0.dma(g_.dma_start(out=d_dbgb[i:i + 1, 0:w], in_=src_))
            for i, src_ in [(0, d_cin[0][0:1, :]), (1, d_cout[0][0:1, :]),
                            (2, d_x1[0, 0:1, :])]:
                S.dsv0.dma(g_.dma_start(out=d_dbgf[i:i + 1, :], in_=src_))
            g_.wait_ge(S.dsv0.h, S.dsv0.n)
        gps(dbgdump)

    def p0c(sy):
        for dst, src in [(s_id32, d_id32), (s_idb, d_idb), (s_mask, d_mask), (s_cb, d_cb),
                         (s_sel2, d_sel2), (s_ones1, d_ones1), (s_onesr, d_onesr),
                         (s_ones2, d_ones2), (s_ropec, d_ropec), (s_ropes, d_ropes),
                         (s_qwp, d_qwp), (s_kwp, d_kwp), (s_gsc, d_gsc), (s_usc, d_usc)]:
            S.dcn.dma(sy.dma_start(out=dst[:], in_=src))
    sync(p0c)
    cap0 = S.dcn.n

    def p0n(v, cap0=cap0):
        v.wait_ge(S.dcn.h, cap0)
        ins = v.tensor_scalar(out=s_ropesn[:], in0=s_ropes[:], scalar1=-1.0, scalar2=None,
                              op0=ALU.mult)
        S.dve.inc(ins)
    vec(p0n)

    p1s = contextlib.ExitStack()
    p1s.__enter__()
    E1 = p1s.enter_context
    s_wT = {nm: [E1(nc.sbuf_tensor(f"s_w{nm}T{h}", [128, KC * 512], BF16)) for h in range(2)]
            for nm in ("q", "k", "v")}

    p0s = contextlib.ExitStack()
    p0s.__enter__()
    E0 = p0s.enter_context
    s_wn = [E0(nc.sbuf_tensor(f"s_wn{i}", [128, D], F32)) for i in range(4)]
    s_wnh = [E0(nc.sbuf_tensor(f"s_wnh{i}", [128, D], BF16)) for i in range(4)]
    s_wnl = [E0(nc.sbuf_tensor(f"s_wnl{i}", [128, D], BF16)) for i in range(4)]
    s_woS = [E0(nc.sbuf_tensor(f"s_woS{hl}", [128, D], BF16)) for hl in range(2)]
    p0b = [E0(nc.psum_tensor(f"p0b{i}", [128, 1024], BF16)) for i in range(2)]

    wprep_rel = {"sp": 0, "tr": 0}

    def prep_qkv(d_src, s_out, kind):
        dcn_prev = S.dcn.n

        def ld(sy, dcn_prev=dcn_prev, relsp=wprep_rel["sp"]):
            sy.wait_ge(S.dcn.h, dcn_prev)
            if relsp:
                sy.wait_ge(S.dve.h, relsp)
            if kind == "qk":
                for p in range(2):
                    for xi in range(2):
                        t = 2 * p + xi
                        S.dcn.dma(sy.dma_start(out=s_wn[t][0:64, :],
                                               in_=d_src[256 * p + xi:256 * p + 128:2, :]))
                        S.dcn.dma(sy.dma_start(out=s_wn[t][64:128, :],
                                               in_=d_src[256 * p + 128 + xi:256 * p + 256:2, :]))
            else:
                for h in range(4):
                    S.dcn.dma(sy.dma_start(out=s_wn[h][:], in_=d_src[128 * h:128 * (h + 1), :]))
        sync(ld)
        c1 = S.dcn.n

        def sp(v, c1=c1, reltr=wprep_rel["tr"]):
            v.wait_ge(S.dcn.h, c1)
            if reltr:
                v.wait_ge(S.pe.h, reltr)
            ins = None
            for t in range(4):
                v.tensor_copy(out=s_wnh[t][:], in_=s_wn[t][:])
                ins = v.tensor_tensor(out=s_wnl[t][:], in0=s_wn[t][:], in1=s_wnh[t][:],
                                      op=ALU.subtract)
            S.dve.inc(ins)
        vec(sp)
        a1 = S.dve.n
        wprep_rel["sp"] = a1
        pe0 = S.pe.n
        act0 = S.act.n

        def tr(t_, a1=a1, act0=act0, pv2=wprep_rel.get("ev2")):
            t_.wait_ge(S.dve.h, a1)
            i = 0
            for hl in range(2):
                srcs = s_wnh if hl == 0 else s_wnl
                for icc in range(KC):
                    for tch in range(4):
                        g, slot = divmod(i, 8)
                        if slot == 0:
                            if g < 2:
                                if pv2 is not None:
                                    t_.wait_ge(S.act.h, pv2[g])
                            else:
                                t_.wait_ge(S.act.h, act0 + g - 1)
                        ins = t_.transpose(out=p0b[g % 2][:, 128 * slot:128 * (slot + 1)],
                                           in_=srcs[tch][:, 128 * icc:128 * (icc + 1)],
                                           identity=s_idb[:])
                        i += 1
                        if slot == 7:
                            S.pe.inc(ins)
        ten(tr)
        wprep_rel["tr"] = S.pe.n

        def ev(s_, pe0=pe0):
            for g in range(16):
                s_.wait_ge(S.pe.h, pe0 + g + 1)
                hl = (8 * g) // (KC * 4)
                ic0 = ((8 * g) % (KC * 4)) // 4
                ins = s_.copy(out=s_out[hl][:, 512 * ic0:512 * ic0 + 1024],
                              in_=p0b[g % 2][:])
                S.act.inc(ins)
        sca(ev)
        wprep_rel["ev2"] = (S.act.n - 1, S.act.n)

    prep_qkv(d_wq, s_wT["q"], "qk")
    prep_qkv(d_wk, s_wT["k"], "qk")
    prep_qkv(d_wv, s_wT["v"], "v")

    prev_spill = 0
    prev_ev = wprep_rel["ev2"]
    for r2 in range(8):
        def ld(sy, r2=r2, pv=prev_spill, dcn_prev=S.dcn.n, relsp=wprep_rel["sp"]):
            sy.wait_ge(S.dcn.h, dcn_prev)
            sy.wait_ge(S.dve.h, relsp)
            if pv:
                sy.wait_ge(S.dwot.h, pv)
            for j in range(2):
                S.dcn.dma(sy.dma_start(out=s_wn[j][:, 0:512],
                                       in_=d_wo[256 * r2 + 128 * j:256 * r2 + 128 * (j + 1), :]))
        sync(ld)
        c1 = S.dcn.n

        def sp(v, c1=c1, reltr=wprep_rel["tr"]):
            v.wait_ge(S.dcn.h, c1)
            v.wait_ge(S.pe.h, reltr)
            ins = None
            for j in range(2):
                v.tensor_copy(out=s_wnh[j][:, 0:512], in_=s_wn[j][:, 0:512])
                ins = v.tensor_tensor(out=s_wnl[j][:, 0:512], in0=s_wn[j][:, 0:512],
                                      in1=s_wnh[j][:, 0:512], op=ALU.subtract)
            S.dve.inc(ins)
        vec(sp)
        a1 = S.dve.n
        wprep_rel["sp"] = a1
        pe0 = S.pe.n

        def tr(t_, a1=a1, pv=prev_ev):
            t_.wait_ge(S.dve.h, a1)
            i = 0
            for hl in range(2):
                srcs = s_wnh if hl == 0 else s_wnl
                for j in range(2):
                    for cc in range(4):
                        g, slot = divmod(i, 8)
                        if slot == 0 and pv is not None:
                            t_.wait_ge(S.act.h, pv[g])
                        ins = t_.transpose(out=p0b[g % 2][:, 128 * slot:128 * (slot + 1)],
                                           in_=srcs[j][:, 128 * cc:128 * (cc + 1)],
                                           identity=s_idb[:])
                        i += 1
                        if slot == 7:
                            S.pe.inc(ins)
        ten(tr)
        wprep_rel["tr"] = S.pe.n
        act_b = S.act.n

        def ev(s_, pe0=pe0):
            for g in range(2):
                s_.wait_ge(S.pe.h, pe0 + g + 1)
                ins = None
                for u in range(8):
                    lin = 8 * g + u
                    hl, rem = divmod(lin, 8)
                    j, cc = divmod(rem, 4)
                    ins = s_.copy(out=s_woS[hl][:, 512 * cc + 128 * j:512 * cc + 128 * (j + 1)],
                                  in_=p0b[g % 2][:, 128 * u:128 * (u + 1)])
                S.act.inc(ins)
        sca(ev)
        prev_ev = [act_b + 1, act_b + 2]
        e1 = S.act.n

        def spl(g_, e1=e1, r2=r2):
            g_.wait_ge(S.act.h, e1)
            for hl in range(2):
                for cc in range(4):
                    S.dwot.dma(g_.dma_start(out=d_wot[cc, hl, :, 256 * r2:256 * (r2 + 1)],
                                            in_=s_woS[hl][:, 512 * cc:512 * cc + 256]))
        gps(spl)
        prev_spill = S.dwot.n

    fin_act = S.act.n
    fin_dwot = S.dwot.n

    def barp0(t_, fin_act=fin_act):
        t_.wait_ge(S.act.h, fin_act)
    ten(barp0)

    def barp0g(g_, fin_dwot=fin_dwot):
        g_.wait_ge(S.dwot.h, fin_dwot)
    gps(barp0g)
    p0s.__exit__(None, None, None)
    if PHASE_CAP == "p0":
        p1s.__exit__(None, None, None)
        st.__exit__(None, None, None)
        return nc

    qjobs = ([("g", rc, cc) for rc in range(16) for cc in range(12)]
             + [("u", rc, cc) for rc in range(16) for cc in range(12)]
             + [("d", rc, cc) for rc in range(IC) for cc in range(4)])
    q8st = {"i": 0, "rel": [0, 0]}

    def emit_qjobs(g_, count):
        for _ in range(count):
            ji = q8st["i"]
            if ji >= len(qjobs):
                return
            q8st["i"] += 1
            kind, rc, cc = qjobs[ji]
            p = ji % 2
            dsem = S.dq8a if p == 0 else S.dq8b
            if q8st["rel"][p]:
                g_.wait_ge(dsem.h, q8st["rel"][p])
            src = {"g": d_gw, "u": d_uw, "d": d_dw}[kind][128 * rc:128 * (rc + 1),
                                                         512 * cc:512 * (cc + 1)]
            nl = dsem.dma(g_.dma_start(out=s_qin[p][:], in_=src))
            g_.wait_ge(dsem.h, nl)
            g_.tensor_copy(out=s_qo8[p][:], in_=s_qin[p][:])
            g_.tensor_copy(out=s_qo8b[p][:].bitcast(U8), in_=s_qo8[p][:].bitcast(U8))
            if kind in ("g", "u"):
                dst_t = d_g8 if kind == "g" else d_u8
                for j in range(4):
                    dsem.dma(g_.dma_start(out=dst_t[4 * cc + j, :, rc, :],
                                          in_=s_qo8b[p][:, 128 * j:128 * (j + 1)]))
            else:
                dsem.dma(g_.dma_start(out=d_d8[rc, :, 512 * cc:512 * (cc + 1)],
                                      in_=s_qo8b[p][:]))
            q8st["rel"][p] = dsem.n

    s_xt = [E1(nc.sbuf_tensor(f"s_xt{i}", [128, D], F32)) for i in range(2)]
    s_tmp = E1(nc.sbuf_tensor("s_tmp", [128, D], F32))
    s_lnh = [E1(nc.sbuf_tensor(f"s_lnh{i}", [128, D], BF16)) for i in range(2)]
    s_lnl = [E1(nc.sbuf_tensor(f"s_lnl{i}", [128, D], BF16)) for i in range(2)]
    s_fmh = [E1(nc.sbuf_tensor(f"s_fmh{i}", [128, KC * BLK], BF16)) for i in range(2)]
    s_fml = [E1(nc.sbuf_tensor(f"s_fml{i}", [128, KC * BLK], BF16)) for i in range(2)]
    s_stat = E1(nc.sbuf_tensor("s_stat", [128, 8], F32))
    s_ksq = [E1(nc.sbuf_tensor(f"s_ksq{i}", [128, BLK], F32)) for i in range(8)]
    s_pw = [E1(nc.sbuf_tensor(f"s_pw{i}", [128, BLK], F32)) for i in range(2)]
    s_pr = [E1(nc.sbuf_tensor(f"s_pr{i}", [128, BLK], F32)) for i in range(2)]
    s_rt = E1(nc.sbuf_tensor("s_rt", [128, BLK], F32))
    s_qn = E1(nc.sbuf_tensor("s_qn", [128, BLK], F32))
    s_sh = [E1(nc.sbuf_tensor(f"s_sh{i}", [128, BLK], BF16)) for i in range(8)]
    s_sl = [E1(nc.sbuf_tensor(f"s_sl{i}", [128, BLK], BF16)) for i in range(8)]
    s_vh = [E1(nc.sbuf_tensor(f"s_vh{i}", [128, 512], BF16)) for i in range(2)]
    s_vl = [E1(nc.sbuf_tensor(f"s_vl{i}", [128, 512], BF16)) for i in range(2)]
    s_rrow = [E1(nc.sbuf_tensor(f"s_rrow{i}", [128, BLK], F32)) for i in range(2)]
    s_rk32 = E1(nc.sbuf_tensor("s_rk32", [128, 128], F32))

    ptb = [E1(nc.psum_tensor(f"ptb{i}", [128, 1024], BF16)) for i in range(2)]
    pKQ = [E1(nc.psum_tensor(f"pKQ{i}", [128, 512], F32)) for i in range(4)]
    pV = E1(nc.psum_tensor("pV", [128, 512], F32))
    pSB = E1(nc.psum_tensor("pSB", [128, 512], F32))

    spsem = [S.dsp0, S.dsp1, S.dsp2, S.dsp3, S.dsp4, S.dsp5, S.dsp6, S.dsp7]
    svsem = [S.dsv0, S.dsv1]

    fin_pe0 = S.pe.n
    fin_act0 = S.act.n
    rel_xt = [0, 0]
    rel_lnhl = 0
    rel_sh = [0] * 8
    rel_sv = [0, 0]
    rel_V1 = 0
    rel_tmp = 0
    prev_fme = None

    NA, ND, NP = 32, 64, 24

    for bi in range(NBLK if NBLK_CAP is None else NBLK_CAP):
        par = bi % 2
        ab, db, pb = S.act.n, S.dve.n, S.pe.n
        dxc = [0, 0]

        for tt in range(2):
            g = 2 * bi + tt
            gp = g % 2
            dxs = S.dx0 if gp == 0 else S.dx1

            first = (bi == 0 and tt == 0)

            def ldx(sy, gp=gp, g=g, dxs=dxs, rx=rel_xt[gp], first=first):
                if first:
                    sy.wait_ge(S.pe.h, fin_pe0)
                    sy.wait_ge(S.act.h, fin_act0)
                    sy.wait_ge(S.dwot.h, fin_dwot)
                if rx:
                    sy.wait_ge(S.dve.h, rx)
                dxs.dma(sy.dma_start(out=s_xt[gp][:], in_=d_x[128 * g:128 * (g + 1), :]))
            sync(ldx)
            dxc[tt] = dxs.n
            rel_xt[gp] = db + 9 * (tt + 1)

        PP = P1_PART if P1_PART is not None else 5

        def act_blk(s_, bi=bi, ab=ab, db=db, pb=pb, par=par, dxc=tuple(dxc),
                    rel_tmp=rel_tmp, PP=None):
            PP = P1_PART if P1_PART is not None else 5
            for tt in range(2):
                gp = (2 * bi + tt) % 2
                dxs = S.dx0 if gp == 0 else S.dx1
                s_.wait_ge(dxs.h, dxc[tt])
                s_.wait_ge(S.dve.h, (db + 9) if tt == 1 else rel_tmp)
                S.act.inc(s_.activation(out=s_tmp[:], in_=s_xt[gp][:], func=ACTF.Square))
                s_.wait_ge(S.dve.h, db + 9 * tt + 5)
                S.act.inc(s_.activation(out=s_stat[:, 6:7], in_=s_stat[:, 5:6],
                                        func=ACTF.Sqrt,
                                        bias=s_cb[:, CB_EPSLN:CB_EPSLN + 1]))
            if PP < 2:
                return
            for g in range(8):
                s_.wait_ge(S.pe.h, pb + 1 + g)
                hl = (8 * g) // (KC * 2)
                ic0 = ((8 * g) % (KC * 2)) // 2
                dst = s_fmh[par] if hl == 0 else s_fml[par]
                S.act.inc(s_.copy(out=dst[:, BLK * ic0:BLK * ic0 + 1024], in_=ptb[g % 2][:]))
            if PP < 4:
                return
            pw_rel = {2: db + 22, 3: db + 22, 4: db + 30, 5: db + 30, 6: db + 44, 7: db + 44}

            def evq(gi, wp):
                p, xi = divmod(gi % 4, 2)
                s_.wait_ge(S.pe.h, pb + 9 + gi)
                if gi in pw_rel:
                    s_.wait_ge(S.dve.h, pw_rel[gi])
                bank = pKQ[gi % 4][:, 256 * (gi // 4):256 * (gi // 4) + BLK]
                if "evq" in KILL:
                    S.act.inc(s_.nop())
                    S.act.inc(s_.nop())
                    return
                S.act.inc(s_.activation(out=s_ksq[gi][:], in_=bank, func=ACTF.Square))
                S.act.inc(s_.activation(out=s_pw[xi][:], in_=bank, func=ACTF.Copy,
                                        scale=wp[:, xi:xi + 1]))

            def rsqrt(i4):
                pe_need = [pb + 19, pb + 20, pb + 21, pb + 22][i4]
                s_.wait_ge(S.pe.h, pe_need)
                if i4 >= 2:
                    s_.wait_ge(S.dve.h, db + 39)
                if i4 < 2:
                    sc_ = s_cb[:, CB_RKS:CB_RKS + 1]
                    bs_ = s_cb[:, CB_RKB:CB_RKB + 1]
                else:
                    sc_ = s_cb[:, CB_RQS:CB_RQS + 1]
                    bs_ = s_cb[:, CB_RQB:CB_RQB + 1]
                S.act.inc(s_.activation(out=s_rrow[i4 % 2][:], in_=pSB[:, 0:BLK],
                                        func=ACTF.Sqrt, scale=sc_, bias=bs_))

            for gi in range(4):
                evq(gi, s_kwp)
            rsqrt(0)
            rsqrt(1)
            for gi in range(4, 8):
                evq(gi, s_qwp)
            rsqrt(2)
            rsqrt(3)
        sca(act_blk)

        def dve_blk(v, bi=bi, ab=ab, db=db, pb=pb, dxc=tuple(dxc),
                    rel_lnhl=rel_lnhl, relsh=tuple(rel_sh), relsv=tuple(rel_sv)):
            PP = P1_PART if P1_PART is not None else 5
            for tt in range(2):
                gp = (2 * bi + tt) % 2
                dxs = S.dx0 if gp == 0 else S.dx1
                v.wait_ge(dxs.h, dxc[tt])
                S.dve.inc(v.reduce_sum(out=s_stat[:, 0:1], in_=s_xt[gp][:], axis=AX.X))
                v.wait_ge(S.act.h, ab + 2 * tt + 1)
                v.reduce_sum(out=s_stat[:, 1:2], in_=s_tmp[:], axis=AX.X)
                S.dve.inc(v.tensor_scalar(out=s_stat[:, 2:3], in0=s_stat[:, 0:1],
                                          scalar1=1.0 / D, scalar2=None, op0=ALU.mult))
                S.dve.inc(v.tensor_scalar(out=s_stat[:, 3:4], in0=s_stat[:, 1:2],
                                          scalar1=1.0 / D, scalar2=None, op0=ALU.mult))
                S.dve.inc(v.tensor_scalar(out=s_stat[:, 4:5], in0=s_stat[:, 2:3],
                                          scalar1=s_stat[:, 2:3], scalar2=-1.0,
                                          op0=ALU.mult, op1=ALU.mult))
                v.tensor_copy(out=s_stat[:, 1:2], in_=s_stat[:, 3:4])
                S.dve.inc(v.tensor_tensor(out=s_stat[:, 5:6], in0=s_stat[:, 3:4],
                                          in1=s_stat[:, 4:5], op=ALU.add))
                v.wait_ge(S.act.h, ab + 2 * tt + 2)
                S.dve.inc(v.reciprocal(out=s_stat[:, 7:8], in_=s_stat[:, 6:7]))
                v.tensor_copy(out=s_stat[:, 1:2], in_=s_stat[:, 5:6])
                S.dve.inc(v.tensor_scalar(out=s_tmp[:], in0=s_xt[gp][:],
                                          scalar1=s_stat[:, 2:3], scalar2=s_stat[:, 7:8],
                                          op0=ALU.subtract, op1=ALU.mult))
                if tt == 0 and rel_lnhl:
                    v.wait_ge(S.pe.h, rel_lnhl)
                S.dve.inc(v.tensor_copy(out=s_lnh[gp][:], in_=s_tmp[:]))
                S.dve.inc(v.tensor_tensor(out=s_lnl[gp][:], in0=s_tmp[:], in1=s_lnh[gp][:],
                                          op=ALU.subtract))

            if PP < 4:
                return

            def rope(p):
                if "rope" in KILL:
                    for _ in range(4):
                        S.dve.inc(v.engine_nop() if hasattr(v, "engine_nop") else v.nop())
                    return
                cc_ = s_ropec[:, p:p + 1]
                ss_ = s_ropes[:, p:p + 1]
                sn_ = s_ropesn[:, p:p + 1]
                S.dve.inc(v.tensor_scalar(out=s_rt[:], in0=s_pw[0][:], scalar1=cc_,
                                          scalar2=None, op0=ALU.mult))
                S.dve.inc(v.scalar_tensor_tensor(out=s_pr[0][:], in0=s_pw[1][:], scalar=sn_,
                                                 in1=s_rt[:], op0=ALU.mult, op1=ALU.add))
                S.dve.inc(v.tensor_scalar(out=s_rt[:], in0=s_pw[1][:], scalar1=cc_,
                                          scalar2=None, op0=ALU.mult))
                S.dve.inc(v.scalar_tensor_tensor(out=s_pr[1][:], in0=s_pw[0][:], scalar=ss_,
                                                 in1=s_rt[:], op0=ALU.mult, op1=ALU.add))

            for p in range(2):
                v.wait_ge(S.act.h, ab + 14 + 2 * (2 * p + 1))
                rope(p)
                for xi in range(2):
                    slot = 2 * p + xi
                    if relsh[slot]:
                        v.wait_ge(spsem[slot].h, relsh[slot])
                    S.dve.inc(v.tensor_copy(out=s_sh[slot][:], in_=s_pr[xi][:]))
                    S.dve.inc(v.tensor_tensor(out=s_sl[slot][:], in0=s_pr[xi][:],
                                              in1=s_sh[slot][:], op=ALU.subtract))
            v.wait_ge(S.pe.h, pb + 17)
            if relsv[0]:
                v.wait_ge(svsem[0].h, relsv[0])
            S.dve.inc(v.tensor_copy(out=s_vh[0][:], in_=pV[:]))
            S.dve.inc(v.tensor_tensor(out=s_vl[0][:], in0=pV[:], in1=s_vh[0][:],
                                      op=ALU.subtract))
            for p in range(2):
                v.wait_ge(S.act.h, ab + 21 + p)
                S.dve.inc(v.reciprocal(out=s_rrow[p % 2][:], in_=s_rrow[p % 2][:]))
            ins = None
            for p in range(2):
                for kt in range(2):
                    for u in range(4):
                        ins = v.transpose(
                            out=s_rk32[32 * u:32 * (u + 1),
                                       32 * (2 * p + kt):32 * (2 * p + kt + 1)],
                            in_=s_rrow[p % 2][0:32, 128 * kt + 32 * u:128 * kt + 32 * (u + 1)])
            S.dve.inc(ins)
            gk0 = 2 * bi
            ins = None
            for p in range(2):
                for kt in range(2):
                    for hh in range(2):
                        ins = v.tensor_copy(
                            out=s_rkc[:, 64 * p + 32 * hh + gk0 + kt:
                                      64 * p + 32 * hh + gk0 + kt + 1],
                            in_=s_rk32[:, 32 * (2 * p + kt) + hh:32 * (2 * p + kt) + hh + 1])
            S.dve.inc(ins)
            for p in range(2):
                v.wait_ge(S.act.h, ab + 26 + 4 * p)
                rope(p)
                v.wait_ge(S.act.h, ab + 31 + p)
                S.dve.inc(v.reciprocal(out=s_rrow[p % 2][:], in_=s_rrow[p % 2][:]))
                v.wait_ge(S.pe.h, pb + 23 + p)
                for xi in range(2):
                    slot = 4 + 2 * p + xi
                    if relsh[slot]:
                        v.wait_ge(spsem[slot].h, relsh[slot])
                    if "qmult" in KILL:
                        S.dve.inc(v.tensor_copy(out=s_qn[:], in_=s_pr[xi][:]))
                    else:
                        S.dve.inc(v.tensor_tensor(out=s_qn[:], in0=pSB[:, 0:BLK],
                                                  in1=s_pr[xi][:], op=ALU.mult))
                    S.dve.inc(v.tensor_copy(out=s_sh[slot][:], in_=s_qn[:]))
                    S.dve.inc(v.tensor_tensor(out=s_sl[slot][:], in0=s_qn[:],
                                              in1=s_sh[slot][:], op=ALU.subtract))
            v.wait_ge(S.pe.h, pb + 18)
            if relsv[1]:
                v.wait_ge(svsem[1].h, relsv[1])
            S.dve.inc(v.tensor_copy(out=s_vh[1][:], in_=pV[:]))
            S.dve.inc(v.tensor_tensor(out=s_vl[1][:], in0=pV[:], in1=s_vh[1][:],
                                      op=ALU.subtract))
        vec(dve_blk)

        fmh, fml = s_fmh[par], s_fml[par]

        def fmt(hl, icc):
            src = fmh if hl == 0 else fml
            return src[:, BLK * icc:BLK * (icc + 1)]

        def pe_blk(t_, bi=bi, ab=ab, db=db, pb=pb, par=par, prev_fme=prev_fme,
                   rel_V1=rel_V1):
            PP = P1_PART if P1_PART is not None else 5
            if PP < 2:
                return
            t_.wait_ge(S.dve.h, db + 18)
            i = 0
            for hl in range(2):
                for icc in range(KC):
                    for tt in range(2):
                        gsrc = (2 * bi + tt) % 2
                        src = s_lnh[gsrc] if hl == 0 else s_lnl[gsrc]
                        g, slot = divmod(i, 8)
                        if slot == 0:
                            if g < 2:
                                if prev_fme is not None:
                                    t_.wait_ge(S.act.h, prev_fme + g)
                            else:
                                t_.wait_ge(S.act.h, ab + 5 + g - 2)
                        ins = t_.transpose(out=ptb[g % 2][:, 128 * slot:128 * (slot + 1)],
                                           in_=src[:, 128 * icc:128 * (icc + 1)],
                                           identity=s_idb[:])
                        i += 1
                        if slot == 7:
                            S.pe.inc(ins)
            if PP < 3:
                return
            t_.wait_ge(S.act.h, ab + 12)
            for proj in range(2):
                wTm = s_wT["k"] if proj == 0 else s_wT["q"]
                for p in range(2):
                    for xi in range(2):
                        gi = 4 * proj + 2 * p + xi
                        ch = 2 * p + xi
                        bank = pKQ[gi % 4][:, 256 * (gi // 4):256 * (gi // 4) + BLK]
                        ins = None
                        for kc in range(KC):
                            for pas in range(3):
                                lh = wTm[0] if pas < 2 else wTm[1]
                                rh = fmt(0 if pas != 1 else 1, kc)
                                ins = t_.matmul(
                                    out=bank,
                                    lhsT=lh[:, 512 * kc + 128 * ch:512 * kc + 128 * (ch + 1)],
                                    rhs=rh, start=(kc == 0 and pas == 0),
                                    stop=(kc == KC - 1 and pas == 2))
                        S.pe.inc(ins)
            for tt in range(2):
                if tt == 0:
                    if rel_V1 and PP >= 4:
                        t_.wait_ge(S.dve.h, rel_V1)
                else:
                    if PP >= 4:
                        t_.wait_ge(S.dve.h, db + 36)
                ins = None
                for kc in range(KC):
                    for pas in range(3):
                        lh = fmt(0 if pas != 1 else 1, kc)[:, 128 * tt:128 * (tt + 1)]
                        rh = (s_wT["v"][0] if pas < 2 else s_wT["v"][1])[:, 512 * kc:512 * (kc + 1)]
                        ins = t_.matmul(out=pV[:], lhsT=lh, rhs=rh,
                                        start=(kc == 0 and pas == 0),
                                        stop=(kc == KC - 1 and pas == 2))
                S.pe.inc(ins)
            if PP < 4:
                return

            def ss_mm(si):
                proj, p = divmod(si, 2)
                gi0 = 4 * proj + 2 * p
                ss_wait = [ab + 15, ab + 21, ab + 25, ab + 31][si]
                t_.wait_ge(S.act.h, ss_wait)
                if "ssmm" in KILL:
                    ins = t_.matmul(out=pSB[0:2, 0:BLK], lhsT=s_ones2[:, 0:2][0:128],
                                    rhs=s_ksq[gi0][:], start=True, stop=True)
                    S.pe.inc(ins)
                    return
                t_.matmul(out=pSB[:, 0:BLK], lhsT=s_ones2[:], rhs=s_ksq[gi0][:],
                          start=True, stop=False)
                ins = t_.matmul(out=pSB[:, 0:BLK], lhsT=s_ones2[:], rhs=s_ksq[gi0 + 1][:],
                                start=False, stop=True)
                S.pe.inc(ins)

            ss_mm(0)
            ss_mm(1)
            ss_mm(2)
            ss_mm(3)
            for p in range(2):
                if p == 0:
                    t_.wait_ge(S.dve.h, db + 45)
                    t_.wait_ge(S.act.h, ab + 32)
                else:
                    t_.wait_ge(S.dve.h, db + 56)
                ins = t_.matmul(out=pSB[:, 0:BLK], lhsT=s_sel2[:],
                                rhs=s_rrow[p % 2][0:2, :], start=True, stop=True)
                S.pe.inc(ins)
        ten(pe_blk)
        rel_lnhl = pb + 8
        rel_V1 = db + 64
        rel_tmp = db + 18
        prev_fme = ab + 11

        if DEBUG_P1:
            fme_now = ab + 12
            lnv_now = [db + 9, db + 18]

            def dbg_blk(g_, bi=bi, par=par, fme_now=fme_now, lnv_now=tuple(lnv_now)):
                for tt in range(2):
                    g = 2 * bi + tt
                    gp = g % 2
                    g_.wait_ge(S.dve.h, lnv_now[tt])
                    g_.dma_start(out=d_dbgln[g, 0], in_=s_lnh[gp][:])
                    g_.dma_start(out=d_dbgln[g, 1], in_=s_lnl[gp][:])
                g_.wait_ge(S.act.h, fme_now)
                g_.dma_start(out=d_dbgfm[bi, 0], in_=s_fmh[par][:])
                ins = g_.dma_start(out=d_dbgfm[bi, 1], in_=s_fml[par][:])
                S.dsv0.dma(ins)
                g_.wait_ge(S.dsv0.h, S.dsv0.n)
            gps(dbg_blk)

        def gps_blk(g_, bi=bi, db=db):
            PP = P1_PART if P1_PART is not None else 5
            if PP < 5:
                if not SKIP_Q8 and PP >= 1:
                    pass
                return
            caps = {("K", 0, 0): db + 24, ("K", 0, 1): db + 26,
                    ("K", 1, 0): db + 32, ("K", 1, 1): db + 34,
                    ("Q", 0, 0): db + 48, ("Q", 0, 1): db + 51,
                    ("Q", 1, 0): db + 59, ("Q", 1, 1): db + 62,
                    ("V", 0): db + 36, ("V", 1): db + 64}
            for proj, dsp_t in (("K", d_ksp), ("Q", d_qsp)):
                for p in range(2):
                    for xi in range(2):
                        slot = (0 if proj == "K" else 4) + 2 * p + xi
                        ssem = spsem[slot]
                        g_.wait_ge(S.dve.h, caps[(proj, p, xi)])
                        if "spk" in KILL:
                            continue
                        for hl in range(2):
                            srcb = s_sh[slot] if hl == 0 else s_sl[slot]
                            ssem.dma(g_.dma_start(
                                out=dsp_t[2 * p, hl, 64 * xi:64 * (xi + 1),
                                          BLK * bi:BLK * (bi + 1)],
                                in_=srcb[0:64, :]))
                            ssem.dma(g_.dma_start(
                                out=dsp_t[2 * p + 1, hl, 64 * xi:64 * (xi + 1),
                                          BLK * bi:BLK * (bi + 1)],
                                in_=srcb[64:128, :]))
                        rel_sh[slot] = ssem.n
                if proj == "K":
                    g_.wait_ge(S.dve.h, caps[("V", 0)])
                    if "spv" not in KILL:
                        for h in range(4):
                            for hl in range(2):
                                srcb = s_vh[0] if hl == 0 else s_vl[0]
                                svsem[0].dma(g_.dma_start(
                                    out=d_vsp[h, hl, :, 2 * bi, :],
                                    in_=srcb[:, 128 * h:128 * (h + 1)]))
                        rel_sv[0] = svsem[0].n
            g_.wait_ge(S.dve.h, caps[("V", 1)])
            if "spv" not in KILL:
                for h in range(4):
                    for hl in range(2):
                        srcb = s_vh[1] if hl == 0 else s_vl[1]
                        svsem[1].dma(g_.dma_start(
                            out=d_vsp[h, hl, :, 2 * bi + 1, :],
                            in_=srcb[:, 128 * h:128 * (h + 1)]))
                rel_sv[1] = svsem[1].n
            if not SKIP_Q8:
                emit_qjobs(g_, 20)
        gps(gps_blk)

        if P1_PART is None:
            assert S.act.n == ab + NA, (bi, S.act.n - ab)
            assert S.dve.n == db + ND, (bi, S.dve.n - db)
            assert S.pe.n == pb + NP, (bi, S.pe.n - pb)

    fin_pe = S.pe.n
    fin_dve = S.dve.n

    def barp1(s_, c=fin_pe):
        s_.wait_ge(S.pe.h, c)
    sca(barp1)

    def barp1v(t_, c=fin_dve):
        t_.wait_ge(S.dve.h, c)
    ten(barp1v)
    p1s.__exit__(None, None, None)
    if PHASE_CAP == "p1":
        emit_dbgdump([(s.h, s.n) for s in (spsem + svsem)])
        st.__exit__(None, None, None)
        return nc

    pas = contextlib.ExitStack()
    pas.__enter__()
    EA = pas.enter_context
    s_kh = [EA(nc.sbuf_tensor(f"s_kh{i}", [128, T], BF16)) for i in range(2)]
    s_kl = [EA(nc.sbuf_tensor(f"s_kl{i}", [128, T], BF16)) for i in range(2)]
    s_avh = [EA(nc.sbuf_tensor(f"s_avh{i}", [128, T], BF16)) for i in range(2)]
    s_avl = [EA(nc.sbuf_tensor(f"s_avl{i}", [128, T], BF16)) for i in range(2)]
    s_qht = [EA(nc.sbuf_tensor(f"s_qht{i}", [128, 1024], BF16)) for i in range(2)]
    s_qlt = [EA(nc.sbuf_tensor(f"s_qlt{i}", [128, 1024], BF16)) for i in range(2)]
    s_woT = [[EA(nc.sbuf_tensor(f"s_woT{c}_{hl}", [128, D], BF16)) for hl in range(2)]
             for c in range(4)]
    s_ttl = [EA(nc.sbuf_tensor(f"s_ttl{i}", [128, 512], F32)) for i in range(2)]
    s_P = [EA(nc.sbuf_tensor(f"s_P{i}", [128, 512], F32)) for i in range(2)]
    s_Ph = [EA(nc.sbuf_tensor(f"s_Ph{i}", [128, 512], BF16)) for i in range(2)]
    s_Pl = [EA(nc.sbuf_tensor(f"s_Pl{i}", [128, 512], BF16)) for i in range(2)]
    s_rec = EA(nc.sbuf_tensor("s_rec", [1, 512], F32))
    s_rbc = EA(nc.sbuf_tensor("s_rbc", [128, 512], F32))
    s_ao = EA(nc.sbuf_tensor("s_ao", [128, 512], F32))
    s_aoh = [EA(nc.sbuf_tensor(f"s_aoh{h}", [128, 1024], BF16)) for h in range(4)]
    s_aol = [EA(nc.sbuf_tensor(f"s_aol{h}", [128, 1024], BF16)) for h in range(4)]
    s_y1 = [EA(nc.sbuf_tensor(f"s_y1s{i}", [128, D], F32)) for i in range(2)]

    psc = [EA(nc.psum_tensor(f"psc{i}", [128, 512], F32)) for i in range(2)]
    pav = EA(nc.psum_tensor("pav", [128, 512], F32))
    psm = EA(nc.psum_tensor("psm", [128, 512], F32))
    pwo = [EA(nc.psum_tensor(f"pwo{i}", [128, 512], F32)) for i in range(4)]

    alsem = [S.dal0, S.dal1]
    rel_al = [0, 0]
    rel_sum = 0
    rel_av = 0
    rel_ao = [0, 0, 0, 0]
    rel_y1 = [0, 0]
    rel_wo = [0, 0, 0, 0]
    rel_Ppack = [0, 0]
    rel_Pf32 = 0
    rel_psc = 0
    rel_rbc = 0
    rel_pwo0 = 0

    spill_tot = [(s.h, s.n) for s in (spsem + svsem)]

    def ldwot(sy, spill_tot=tuple(spill_tot)):
        for hh, nn in spill_tot:
            sy.wait_ge(hh, nn)
        for c in range(4):
            for hl in range(2):
                S.dal0.dma(sy.dma_start(out=s_woT[c][hl][:], in_=d_wot[c, hl, :, :]))
    sync(ldwot)
    wot_cap = S.dal0.n

    def wotwait(t_, wot_cap=wot_cap):
        t_.wait_ge(S.dal0.h, wot_cap)
    ten(wotwait)

    for qb in range(4):
        ctx = 1024 * (qb + 1)
        for h in range(4):
            it = 4 * qb + h
            par = it % 2
            asem = alsem[par]

            def lda(sy, h=h, qb=qb, ctx=ctx, par=par, asem=asem, rel=rel_al[par]):
                if rel:
                    sy.wait_ge(S.pe.h, rel)
                asem.dma(sy.dma_start(out=s_kh[par][:, 0:ctx], in_=d_ksp[h, 0, :, 0:ctx]))
                asem.dma(sy.dma_start(out=s_kl[par][:, 0:ctx], in_=d_ksp[h, 1, :, 0:ctx]))
                asem.dma(sy.dma_start(
                    out=s_avh[par][:, 0:ctx].rearrange("p (j d) -> p j d", d=128),
                    in_=d_vsp[h, 0, :, 0:ctx // 128, :]))
                asem.dma(sy.dma_start(
                    out=s_avl[par][:, 0:ctx].rearrange("p (j d) -> p j d", d=128),
                    in_=d_vsp[h, 1, :, 0:ctx // 128, :]))
                asem.dma(sy.dma_start(out=s_qht[par][:],
                                      in_=d_qsp[h, 0, :, 1024 * qb:1024 * (qb + 1)]))
                asem.dma(sy.dma_start(out=s_qlt[par][:],
                                      in_=d_qsp[h, 1, :, 1024 * qb:1024 * (qb + 1)]))
            sync(lda)
            a_cap = asem.n

            for qs in range(2):
                q0 = 1024 * qb + 512 * qs
                nk = q0 // 128 + 4
                diag0 = nk - 4
                pbase, abase, dbase = S.pe.n, S.act.n, S.dve.n
                dve_mask_cap = {}
                dve_split_cap = {}
                c = dbase
                for j in range(nk):
                    if j >= diag0:
                        c += 1
                        dve_mask_cap[j] = c
                    c += 1
                    dve_split_cap[j] = c
                dve_end = c
                pe_sc_cap = {j: pbase + 2 * j + 1 for j in range(nk)}
                pe_av_cap = {j: pbase + 2 * j + 2 for j in range(nk)}
                pe_bc_cap = pbase + 2 * nk + 1

                def pe_qs(t_, h=h, qs=qs, nk=nk, par=par, a_cap=a_cap, asem=asem,
                          abase=abase, dsc=dict(dve_split_cap), dve_end=dve_end,
                          rel_sum=rel_sum, rel_av=rel_av, rel_psc=rel_psc,
                          rel_rbc=rel_rbc, rel_pwo0=rel_pwo0):
                    t_.wait_ge(asem.h, a_cap)

                    def sc(j):
                        if j < 2:
                            if rel_psc:
                                t_.wait_ge(S.act.h, rel_psc)
                        else:
                            t_.wait_ge(S.act.h, abase + j - 1)
                        ins = None
                        for pas in range(3):
                            lh = (s_kh[par] if pas < 2 else s_kl[par])[:, 128 * j:128 * (j + 1)]
                            rh = (s_qht[par] if pas != 1 else s_qlt[par])[:, 512 * qs:512 * (qs + 1)]
                            ins = t_.matmul(out=psc[j % 2][:], lhsT=lh, rhs=rh,
                                            start=(pas == 0), stop=(pas == 2))
                        S.pe.inc(ins)

                    def av(j):
                        t_.wait_ge(S.dve.h, dsc[j])
                        if j == 0:
                            if rel_sum:
                                t_.wait_ge(S.dve.h, rel_sum)
                            if rel_av:
                                t_.wait_ge(S.dve.h, rel_av)
                        t_.matmul(out=psm[0:1, 0:512], lhsT=s_ones1[:], rhs=s_Ph[j % 2][:],
                                  start=(j == 0), stop=False)
                        t_.matmul(out=psm[0:1, 0:512], lhsT=s_ones1[:], rhs=s_Pl[j % 2][:],
                                  start=False, stop=(j == nk - 1))
                        ins = None
                        for pas in range(3):
                            lh = (s_avh[par] if pas < 2 else s_avl[par])[:, 128 * j:128 * (j + 1)]
                            rh = s_Ph[j % 2][:] if pas != 1 else s_Pl[j % 2][:]
                            ins = t_.matmul(out=pav[:], lhsT=lh, rhs=rh,
                                            start=(j == 0 and pas == 0),
                                            stop=(j == nk - 1 and pas == 2))
                        S.pe.inc(ins)

                    for j in range(nk):
                        sc(j)
                        if j >= 1:
                            av(j - 1)
                    av(nk - 1)
                    t_.wait_ge(S.dve.h, dve_end + 1)
                    if rel_rbc:
                        t_.wait_ge(S.act.h, rel_rbc)
                    if rel_pwo0:
                        t_.wait_ge(S.dve.h, rel_pwo0)
                    ins = t_.matmul(out=pwo[0][:], lhsT=s_onesr[:], rhs=s_rec[:],
                                    start=True, stop=True)
                    S.pe.inc(ins)
                ten(pe_qs)

                def act_qs(s_, h=h, nk=nk, par=par, abase=abase, diag0=diag0,
                           pe_sc_cap=dict(pe_sc_cap), dmc=dict(dve_mask_cap),
                           dsc=dict(dve_split_cap), pe_bc_cap=pe_bc_cap,
                           relPf=rel_Pf32):
                    for j in range(nk):
                        if j >= diag0:
                            s_.wait_ge(S.dve.h, dmc[j])
                        else:
                            s_.wait_ge(S.pe.h, pe_sc_cap[j])
                        if j >= 2:
                            s_.wait_ge(S.dve.h, dsc[j - 2])
                        elif relPf:
                            s_.wait_ge(S.dve.h, relPf)
                        s_.activation(out=s_ttl[j % 2][:], in_=psc[j % 2][:], func=ACTF.Tanh,
                                      scale=s_rkc[:, 32 * h + j:32 * h + j + 1])
                        S.act.inc(s_.activation(out=s_P[j % 2][:], in_=s_ttl[j % 2][:],
                                                func=ACTF.Exp,
                                                scale=s_cb[:, CB_EXPS:CB_EXPS + 1],
                                                bias=s_cb[:, CB_EXPB:CB_EXPB + 1]))
                    s_.wait_ge(S.pe.h, pe_bc_cap)
                    S.act.inc(s_.copy(out=s_rbc[:], in_=pwo[0][:]))
                sca(act_qs)
                act_rbc_cap = S.act.n

                def dve_qs(v, h=h, qs=qs, nk=nk, q0=q0, par=par, abase=abase, diag0=diag0,
                           pe_sc_cap=dict(pe_sc_cap), pe_av_cap=dict(pe_av_cap),
                           relPk=tuple(rel_Ppack), act_rbc_cap=act_rbc_cap,
                           rel_aoh=rel_ao[h]):
                    for j in range(nk):
                        if j >= diag0:
                            v.wait_ge(S.pe.h, pe_sc_cap[j])
                            dsh = 384 + q0 - 128 * j
                            S.dve.inc(v.tensor_tensor(out=psc[j % 2][:], in0=psc[j % 2][:],
                                                      in1=s_mask[:, dsh:dsh + 512],
                                                      op=ALU.add))
                        v.wait_ge(S.act.h, abase + j + 1)
                        if j >= 2:
                            v.wait_ge(S.pe.h, pe_av_cap[j - 2])
                        elif relPk[j % 2]:
                            v.wait_ge(S.pe.h, relPk[j % 2])
                        v.tensor_copy(out=s_Ph[j % 2][:], in_=s_P[j % 2][:])
                        S.dve.inc(v.tensor_tensor(out=s_Pl[j % 2][:], in0=s_P[j % 2][:],
                                                  in1=s_Ph[j % 2][:], op=ALU.subtract))
                    v.wait_ge(S.pe.h, pe_av_cap[nk - 1])
                    S.dve.inc(v.reciprocal(out=s_rec[:], in_=psm[0:1, 0:512]))
                    v.wait_ge(S.act.h, act_rbc_cap)
                    S.dve.inc(v.tensor_tensor(out=s_ao[:], in0=pav[:], in1=s_rbc[:],
                                              op=ALU.mult))
                    if rel_aoh:
                        v.wait_ge(S.pe.h, rel_aoh)
                    S.dve.inc(v.tensor_copy(out=s_aoh[h][:, 512 * qs:512 * (qs + 1)],
                                            in_=s_ao[:]))
                    S.dve.inc(v.tensor_tensor(out=s_aol[h][:, 512 * qs:512 * (qs + 1)],
                                              in0=s_ao[:],
                                              in1=s_aoh[h][:, 512 * qs:512 * (qs + 1)],
                                              op=ALU.subtract))
                vec(dve_qs)
                rel_sum = dve_end + 1
                rel_av = dve_end + 2
                rel_pwo0 = dve_end + 2
                rel_Ppack = [pe_av_cap[nk - 2], pe_av_cap[nk - 1]]
                rel_Pf32 = dve_split_cap[nk - 1]
                rel_psc = abase + nk
                rel_rbc = act_rbc_cap
                if qs == 1:
                    rel_al[par] = pe_av_cap[nk - 1]

            def gq(g_):
                emit_qjobs(g_, 16)
            gps(gq)

        ao_all = S.dve.n
        dve_wo_base = S.dve.n
        pe_wo_base = S.pe.n

        def wo_pe(t_, ao_all=ao_all, dve_wo_base=dve_wo_base,
                  relwo=tuple(rel_wo)):
            t_.wait_ge(S.dve.h, ao_all)
            for tt in range(8):
                for ob in range(4):
                    if tt == 0:
                        if relwo[ob]:
                            t_.wait_ge(S.dve.h, relwo[ob])
                    else:
                        t_.wait_ge(S.dve.h, dve_wo_base + 4 * (tt - 1) + ob + 1)
                    ins = None
                    for hd in range(4):
                        for pas in range(3):
                            lh = (s_aoh[hd] if pas < 2 else s_aol[hd])[:, 128 * tt:128 * (tt + 1)]
                            rh = (s_woT[hd][0] if pas != 1 else s_woT[hd][1])[:, 512 * ob:512 * (ob + 1)]
                            ins = t_.matmul(out=pwo[ob][:], lhsT=lh, rhs=rh,
                                            start=(hd == 0 and pas == 0),
                                            stop=(hd == 3 and pas == 2))
                    S.pe.inc(ins)
        ten(wo_pe)
        y1sem = [S.dy1, S.dy1b]
        y1_caps = [y1sem[tt % 2].n + 16 * (tt // 2 + 1) for tt in range(8)]

        def wo_dve(v, pe_wo_base=pe_wo_base, rel_y1t=tuple(rel_y1), y1_caps=tuple(y1_caps)):
            for tt in range(8):
                sl = tt % 2
                for ob in range(4):
                    v.wait_ge(S.pe.h, pe_wo_base + 4 * tt + ob + 1)
                    if ob == 0:
                        if tt < 2:
                            if rel_y1t[sl]:
                                v.wait_ge(y1sem[sl].h, rel_y1t[sl])
                        else:
                            v.wait_ge(y1sem[sl].h, y1_caps[tt - 2])
                    S.dve.inc(v.tensor_copy(out=s_y1[sl][:, 512 * ob:512 * (ob + 1)],
                                            in_=pwo[ob][:]))
        vec(wo_dve)

        def wo_gps(g_, qb=qb, dve_wo_base=dve_wo_base):
            for tt in range(8):
                g_.wait_ge(S.dve.h, dve_wo_base + 4 * (tt + 1))
                y1sem[tt % 2].dma(g_.dma_start(out=d_cin[qb][128 * tt:128 * (tt + 1), :],
                                               in_=s_y1[tt % 2][:]))
            g_.wait_ge(S.dy1.h, S.dy1.n)
            g_.wait_ge(S.dy1b.h, S.dy1b.n)
            ins = g_.collective_compute(
                "ReduceScatter", ALU.add, ins=[d_cin[qb]], outs=[d_cout[qb]],
                replica_groups=[[0, 1, 2, 3], [4, 5, 6, 7]])
            S.cc.inc(ins)
        gps(wo_gps)
        rel_y1 = [y1_caps[6], y1_caps[7]]
        rel_wo = [dve_wo_base + 29, dve_wo_base + 30, dve_wo_base + 31, dve_wo_base + 32]
        for h in range(4):
            rel_ao[h] = S.pe.n
        rel_pwo0 = dve_wo_base + 29

    fin_pe2 = S.pe.n
    fin_dve2 = S.dve.n

    def barpa(s_, c=fin_pe2):
        s_.wait_ge(S.pe.h, c)
    sca(barpa)

    def barpa2(t_, c=fin_dve2):
        t_.wait_ge(S.dve.h, c)
    ten(barpa2)
    pas.__exit__(None, None, None)
    if PHASE_CAP == "pa":
        emit_dbgdump([(S.cc.h, S.cc.n)])
        st.__exit__(None, None, None)
        return nc

    p2a = contextlib.ExitStack()
    p2a.__enter__()
    E2 = p2a.enter_context
    s_y1t = [E2(nc.sbuf_tensor(f"s_y1t{i}", [128, D], F32)) for i in range(2)]
    s_xr = [E2(nc.sbuf_tensor(f"s_xr{i}", [128, D], F32)) for i in range(2)]
    s_x1t = [E2(nc.sbuf_tensor(f"s_x1t{i}", [128, D], F32)) for i in range(2)]
    s_sq2 = E2(nc.sbuf_tensor("s_sq2", [128, D], F32))
    s_st2 = E2(nc.sbuf_tensor("s_st2", [128, 8], F32))
    s_t2 = E2(nc.sbuf_tensor("s_t2", [128, D], F32))
    s_l3h = [E2(nc.sbuf_tensor(f"s_l3h{i}", [128, D], BF16)) for i in range(4)]
    s_l3l = [E2(nc.sbuf_tensor(f"s_l3l{i}", [128, D], BF16)) for i in range(4)]
    s_fm2h = E2(nc.sbuf_tensor("s_fm2h", [128, KC * 512], BF16))
    s_fm2l = E2(nc.sbuf_tensor("s_fm2l", [128, KC * 512], BF16))
    s_gu8 = [E2(nc.sbuf_tensor(f"s_gu8{i}", [128, 4096], FP8)) for i in range(2)]
    s_gt = E2(nc.sbuf_tensor("s_gt", [128, 512], F32))
    s_ut = E2(nc.sbuf_tensor("s_ut", [128, 512], F32))
    s_fu = E2(nc.sbuf_tensor("s_fu", [128, 512], F32))
    s_fh = [E2(nc.sbuf_tensor(f"s_fh{i}", [128, 512], BF16)) for i in range(2)]
    s_fl = [E2(nc.sbuf_tensor(f"s_fl{i}", [128, 512], BF16)) for i in range(2)]

    ptb2 = [E2(nc.psum_tensor(f"ptb2{i}", [128, 1024], BF16)) for i in range(2)]
    pgu = [E2(nc.psum_tensor(f"pgu{i}", [128, 512], F32)) for i in range(4)]

    gusem = [S.dgu0, S.dgu1]
    fssem = [S.dfs0, S.dfs1]
    rel_gu = [0, 0]
    rel_fs = [0, 0]
    rel_y1sl = [0, 0]
    rel_x1sl = [0, 0]
    fme2_plan = None
    q8_dma_counts = [0, 0]
    for ji, (kind, rc, cc) in enumerate(qjobs):
        q8_dma_counts[ji % 2] += 16 * (1 + (4 if kind in ("g", "u") else 1))

    for half in range(2):
        tiles = [4 * half + i for i in range(4)]
        l3_done = 0
        for i4, gt in enumerate(tiles):
            ch, jj = divmod(gt, 2)
            sl = i4 % 2
            db2 = S.dve.n
            ab2 = S.act.n

            dylsem = S.dyl if sl == 0 else S.dylb
            dxrsem = S.dxr if sl == 0 else S.dxrb

            def ldy(sy, ch=ch, jj=jj, gt=gt, sl=sl, half=half, i4=i4,
                    rely=rel_y1sl[sl], dylsem=dylsem, dxrsem=dxrsem):
                if half == 0 and i4 == 0:
                    sy.wait_ge(S.cc.h, 4)
                    sy.wait_ge(S.pe.h, fin_pe2)
                    sy.wait_ge(S.dve.h, fin_dve2)
                else:
                    sy.wait_ge(S.cc.h, 2 * (half + 1))
                if rely:
                    sy.wait_ge(S.dve.h, rely)
                dylsem.dma(sy.dma_start(out=s_y1t[sl][:],
                                        in_=d_cout[ch][128 * jj:128 * (jj + 1), :]))
                dxrsem.dma(sy.dma_start(out=s_xr[sl][:],
                                        in_=d_xres[128 * gt:128 * (gt + 1), :]))
            sync(ldy)
            dylc, dxrc = dylsem.n, dxrsem.n

            def p2act(s_, sl=sl, dylc=dylc, db2=db2, dylsem=dylsem):
                s_.wait_ge(dylsem.h, dylc)
                s_.wait_ge(S.dve.h, db2)
                S.act.inc(s_.activation(out=s_sq2[:], in_=s_y1t[sl][:], func=ACTF.Square))
                s_.wait_ge(S.dve.h, db2 + 5)
                S.act.inc(s_.activation(out=s_st2[:, 6:7], in_=s_st2[:, 5:6], func=ACTF.Sqrt,
                                        bias=s_cb[:, CB_EPSLN:CB_EPSLN + 1]))
                s_.wait_ge(S.dve.h, db2 + 8)
                S.act.inc(s_.activation(out=s_sq2[:], in_=s_x1t[sl][:], func=ACTF.Square))
                s_.wait_ge(S.dve.h, db2 + 13)
                S.act.inc(s_.activation(out=s_st2[:, 6:7], in_=s_st2[:, 5:6], func=ACTF.Sqrt,
                                        bias=s_cb[:, CB_EPSLN:CB_EPSLN + 1]))
            sca(p2act)

            def p2dve(v, sl=sl, i4=i4, dylc=dylc, dxrc=dxrc, ab2=ab2, db2=db2,
                      relx=rel_x1sl[sl], dylsem=dylsem, dxrsem=dxrsem):
                def stats(src, sqcap):
                    S.dve.inc(v.reduce_sum(out=s_st2[:, 0:1], in_=src[:], axis=AX.X))
                    v.wait_ge(S.act.h, sqcap)
                    v.reduce_sum(out=s_st2[:, 1:2], in_=s_sq2[:], axis=AX.X)
                    S.dve.inc(v.tensor_scalar(out=s_st2[:, 2:3], in0=s_st2[:, 0:1],
                                              scalar1=1.0 / D, scalar2=None, op0=ALU.mult))
                    S.dve.inc(v.tensor_scalar(out=s_st2[:, 3:4], in0=s_st2[:, 1:2],
                                              scalar1=1.0 / D, scalar2=None, op0=ALU.mult))
                    S.dve.inc(v.tensor_scalar(out=s_st2[:, 4:5], in0=s_st2[:, 2:3],
                                              scalar1=s_st2[:, 2:3], scalar2=-1.0,
                                              op0=ALU.mult, op1=ALU.mult))
                    v.tensor_copy(out=s_st2[:, 1:2], in_=s_st2[:, 3:4])
                    S.dve.inc(v.tensor_tensor(out=s_st2[:, 5:6], in0=s_st2[:, 3:4],
                                              in1=s_st2[:, 4:5], op=ALU.add))
                v.wait_ge(dylsem.h, dylc)
                stats(s_y1t[sl], ab2 + 1)
                v.wait_ge(S.act.h, ab2 + 2)
                S.dve.inc(v.reciprocal(out=s_st2[:, 7:8], in_=s_st2[:, 6:7]))
                v.tensor_copy(out=s_st2[:, 1:2], in_=s_st2[:, 5:6])
                S.dve.inc(v.tensor_scalar(out=s_t2[:], in0=s_y1t[sl][:],
                                          scalar1=s_st2[:, 2:3], scalar2=s_st2[:, 7:8],
                                          op0=ALU.subtract, op1=ALU.mult))
                v.wait_ge(dxrsem.h, dxrc)
                if relx:
                    v.wait_ge(S.dx1s.h, relx)
                S.dve.inc(v.tensor_tensor(out=s_x1t[sl][:], in0=s_t2[:], in1=s_xr[sl][:],
                                          op=ALU.add))
                stats(s_x1t[sl], ab2 + 3)
                v.wait_ge(S.act.h, ab2 + 4)
                S.dve.inc(v.reciprocal(out=s_st2[:, 7:8], in_=s_st2[:, 6:7]))
                v.tensor_copy(out=s_st2[:, 1:2], in_=s_st2[:, 5:6])
                S.dve.inc(v.tensor_scalar(out=s_t2[:], in0=s_x1t[sl][:],
                                          scalar1=s_st2[:, 2:3], scalar2=s_st2[:, 7:8],
                                          op0=ALU.subtract, op1=ALU.mult))
                S.dve.inc(v.tensor_copy(out=s_l3h[i4][:], in_=s_t2[:]))
                S.dve.inc(v.tensor_tensor(out=s_l3l[i4][:], in0=s_t2[:], in1=s_l3h[i4][:],
                                          op=ALU.subtract))
            vec(p2dve)
            rel_y1sl[sl] = db2 + 8
            l3_done = S.dve.n

            def p2sp(g_, gt=gt, sl=sl, db2=db2):
                g_.wait_ge(S.dve.h, db2 + 8)
                S.dx1s.dma(g_.dma_start(out=d_x1[gt], in_=s_x1t[sl][:]))
            gps(p2sp)
            rel_x1sl[sl] = S.dx1s.n

        pe2 = S.pe.n

        def tr2(t_, l3_done=l3_done):
            t_.wait_ge(S.dve.h, l3_done)
            i = 0
            for hl in range(2):
                for icc in range(KC):
                    for t4 in range(4):
                        g, slot = divmod(i, 8)
                        if slot == 0 and g >= 2:
                            t_.wait_ge(S.act.h, fme2_plan[g - 2])
                        src = s_l3h[t4] if hl == 0 else s_l3l[t4]
                        ins = t_.transpose(out=ptb2[g % 2][:, 128 * slot:128 * (slot + 1)],
                                           in_=src[:, 128 * icc:128 * (icc + 1)],
                                           identity=s_idb[:])
                        i += 1
                        if slot == 7:
                            S.pe.inc(ins)
        fme2_plan = [S.act.n + g + 1 for g in range(16)]
        ten(tr2)

        def ev2(s_, pe2=pe2):
            for g in range(16):
                s_.wait_ge(S.pe.h, pe2 + g + 1)
                hl = (8 * g) // (KC * 4)
                ic0 = ((8 * g) % (KC * 4)) // 4
                dst = s_fm2h if hl == 0 else s_fm2l
                S.act.inc(s_.copy(out=dst[:, 512 * ic0:512 * ic0 + 1024], in_=ptb2[g % 2][:]))
        sca(ev2)
        fm2_ready = S.act.n

        pe_f1 = S.pe.n
        act_f1 = S.act.n
        dve_f1 = S.dve.n
        for ii in range(IC):
            par2 = ii % 2
            gsem = gusem[par2]

            def ldgu(sy, ii=ii, par2=par2, gsem=gsem, relg=rel_gu[par2], half=half):
                if half == 0 and ii < 2:
                    sy.wait_ge(S.dq8a.h, q8_dma_counts[0])
                    sy.wait_ge(S.dq8b.h, q8_dma_counts[1])
                if relg:
                    sy.wait_ge(S.pe.h, relg)
                gsem.dma(sy.dma_start(
                    out=s_gu8[par2][:, 0:2048].rearrange("p (k i) -> p k i", i=128),
                    in_=d_g8[ii]))
                gsem.dma(sy.dma_start(
                    out=s_gu8[par2][:, 2048:4096].rearrange("p (k i) -> p k i", i=128),
                    in_=d_u8[ii]))
            sync(ldgu)
            gcap = gsem.n
            pe_i = S.pe.n

            def f1pe(t_, ii=ii, par2=par2, gsem=gsem, gcap=gcap, fm2_ready=fm2_ready,
                     reli=(act_f1 + ii - 1, dve_f1 + 4 * (ii - 2) + 4)):
                t_.wait_ge(gsem.h, gcap)
                if ii == 0:
                    t_.wait_ge(S.act.h, fm2_ready)
                if ii >= 2:
                    t_.wait_ge(S.act.h, reli[0])
                    t_.wait_ge(S.dve.h, reli[1])
                for gu in range(2):
                    bank = pgu[2 * par2 + gu]
                    ins = None
                    for kc in range(KC):
                        for hl in range(2):
                            lh = s_gu8[par2][:, 2048 * gu + 128 * kc:2048 * gu + 128 * (kc + 1)]
                            rh = (s_fm2h if hl == 0 else s_fm2l)[:, 512 * kc:512 * (kc + 1)]
                            ins = t_.matmul(out=bank[:], lhsT=lh, rhs=rh,
                                            start=(kc == 0 and hl == 0),
                                            stop=(kc == KC - 1 and hl == 1))
                    S.pe.inc(ins)
            ten(f1pe)

            def f1act(s_, ii=ii, par2=par2, pe_i=pe_i):
                s_.wait_ge(S.pe.h, pe_i + 1)
                S.act.inc(s_.activation(out=s_gt[:], in_=pgu[2 * par2][:], func=ACTF.Silu,
                                        scale=s_gsc[:, ii:ii + 1]))
            sca(f1act)
            acap = S.act.n

            def f1dve(v, ii=ii, par2=par2, pe_i=pe_i, acap=acap, relf=rel_fs[par2]):
                v.wait_ge(S.pe.h, pe_i + 2)
                S.dve.inc(v.tensor_scalar(out=s_ut[:], in0=pgu[2 * par2 + 1][:],
                                          scalar1=s_usc[:, ii:ii + 1], scalar2=None,
                                          op0=ALU.mult))
                v.wait_ge(S.act.h, acap)
                S.dve.inc(v.tensor_tensor(out=s_fu[:], in0=s_gt[:], in1=s_ut[:],
                                          op=ALU.mult))
                if relf:
                    v.wait_ge(fssem[par2].h, relf)
                S.dve.inc(v.tensor_copy(out=s_fh[par2][:], in_=s_fu[:]))
                S.dve.inc(v.tensor_tensor(out=s_fl[par2][:], in0=s_fu[:], in1=s_fh[par2][:],
                                          op=ALU.subtract))
            vec(f1dve)
            dcap = S.dve.n

            def f1gp(g_, ii=ii, par2=par2, dcap=dcap, half=half):
                g_.wait_ge(S.dve.h, dcap)
                fssem[par2].dma(g_.dma_start(out=d_fsp[ii, 0, :, 512 * half:512 * (half + 1)],
                                             in_=s_fh[par2][:]))
                fssem[par2].dma(g_.dma_start(out=d_fsp[ii, 1, :, 512 * half:512 * (half + 1)],
                                             in_=s_fl[par2][:]))
            gps(f1gp)
            rel_fs[par2] = fssem[par2].n
            rel_gu[par2] = S.pe.n

    fin_pe3 = S.pe.n
    fin_dve3 = S.dve.n
    fin_fs = [fssem[0].n, fssem[1].n]

    def barp2(s_, c=fin_pe3):
        s_.wait_ge(S.pe.h, c)
    sca(barp2)

    def barp2v(t_, c=fin_dve3):
        t_.wait_ge(S.dve.h, c)
    ten(barp2v)
    p2a.__exit__(None, None, None)
    if PHASE_CAP == "p2a":
        emit_dbgdump([(S.dx1s.h, S.dx1s.n), (S.dfs0.h, S.dfs0.n), (S.dfs1.h, S.dfs1.n)])
        st.__exit__(None, None, None)
        return nc

    p2b = contextlib.ExitStack()
    p2b.__enter__()
    E3 = p2b.enter_context
    s_fq = E3(nc.sbuf_tensor("s_fq", [128, IC * 2 * 256], BF16))
    s_d8q = E3(nc.sbuf_tensor("s_d8q", [128, IC * 1024], FP8))
    s_y2 = [E3(nc.sbuf_tensor(f"s_y2{i}", [128, D], F32)) for i in range(2)]
    s_x1r = [E3(nc.sbuf_tensor(f"s_x1r{i}", [128, D], F32)) for i in range(2)]
    s_yo = [E3(nc.sbuf_tensor(f"s_yo{i}", [128, D], F32)) for i in range(2)]
    s_t4 = E3(nc.sbuf_tensor("s_t4", [128, D], F32))
    s_sq4 = E3(nc.sbuf_tensor("s_sq4", [128, D], F32))
    s_st4 = E3(nc.sbuf_tensor("s_st4", [128, 8], F32))
    s_dsr = E3(nc.sbuf_tensor("s_dsr", [1, D], F32))
    s_dsb = E3(nc.sbuf_tensor("s_dsb", [128, D], F32))
    pgu2 = [E3(nc.psum_tensor(f"pgu2{i}", [128, 512], F32)) for i in range(4)]
    pbc2 = E3(nc.psum_tensor("pbc2", [128, 512], F32))

    def lddsc(sy, ffs=tuple(fin_fs)):
        sy.wait_ge(S.dfs0.h, ffs[0])
        sy.wait_ge(S.dfs1.h, ffs[1])
        sy.wait_ge(S.pe.h, fin_pe3)
        sy.wait_ge(S.dve.h, fin_dve3)
        S.ddsc.dma(sy.dma_start(out=s_dsr[:], in_=d_dsc))
    sync(lddsc)
    dsc_cap = S.ddsc.n
    pe_dsc0 = S.pe.n
    act_dsc0 = S.act.n

    def bcd(t_, dsc_cap=dsc_cap, act_dsc0=act_dsc0):
        t_.wait_ge(S.ddsc.h, dsc_cap)
        for j in range(4):
            if j > 0:
                t_.wait_ge(S.act.h, act_dsc0 + j)
            ins = t_.matmul(out=pbc2[:], lhsT=s_onesr[:], rhs=s_dsr[:, 512 * j:512 * (j + 1)],
                            start=True, stop=True)
            S.pe.inc(ins)
    ten(bcd)

    def bcde(s_, pe_dsc0=pe_dsc0):
        for j in range(4):
            s_.wait_ge(S.pe.h, pe_dsc0 + j + 1)
            S.act.inc(s_.copy(out=s_dsb[:, 512 * j:512 * (j + 1)], in_=pbc2[:]))
    sca(bcde)
    dsb_cap = S.act.n

    rel_fq = 0
    rel_d8 = 0
    rel_x1r = [0, 0]
    rel_yo = [0, 0]
    evac_caps = {}

    for q in range(4):
        gts = [2 * q, 2 * q + 1]

        def ldfq(sy, q=q, rel_fq=rel_fq, ffs=tuple(fin_fs)):
            if q == 0:
                sy.wait_ge(S.dfs0.h, ffs[0])
                sy.wait_ge(S.dfs1.h, ffs[1])
            if rel_fq:
                sy.wait_ge(S.pe.h, rel_fq)
            for ii in range(IC):
                for hl in range(2):
                    S.dfl.dma(sy.dma_start(
                        out=s_fq[:, (2 * ii + hl) * 256:(2 * ii + hl + 1) * 256],
                        in_=d_fsp[ii, hl, :, 256 * q:256 * (q + 1)]))
        sync(ldfq)
        fq_cap = S.dfl.n

        for oh in range(2):
            def ldd8(sy, oh=oh, rel_d8=rel_d8):
                if rel_d8:
                    sy.wait_ge(S.pe.h, rel_d8)
                for ii in range(IC):
                    S.dd8.dma(sy.dma_start(out=s_d8q[:, 1024 * ii:1024 * (ii + 1)],
                                           in_=d_d8[ii, :, 1024 * oh:1024 * (oh + 1)]))
            sync(ldd8)
            d8_cap = S.dd8.n
            pe_q = S.pe.n

            def f2pe(t_, q=q, oh=oh, fq_cap=fq_cap, d8_cap=d8_cap):
                t_.wait_ge(S.dfl.h, fq_cap)
                t_.wait_ge(S.dd8.h, d8_cap)
                for tt2 in range(2):
                    for osb in range(2):
                        gidx = 2 * tt2 + osb
                        prev_e = evac_caps.get((tt2, osb))
                        if prev_e:
                            t_.wait_ge(S.dve.h, prev_e)
                        ins = None
                        for ii in range(IC):
                            for hl in range(2):
                                lh = s_fq[:, (2 * ii + hl) * 256 + 128 * tt2:
                                          (2 * ii + hl) * 256 + 128 * (tt2 + 1)]
                                rh = s_d8q[:, 1024 * ii + 512 * osb:1024 * ii + 512 * (osb + 1)]
                                ins = t_.matmul(out=pgu2[gidx][:], lhsT=lh, rhs=rh,
                                                start=(ii == 0 and hl == 0),
                                                stop=(ii == IC - 1 and hl == 1))
                        S.pe.inc(ins)
            ten(f2pe)

            def f2dve(v, q=q, oh=oh, pe_q=pe_q, dsb_cap=dsb_cap):
                v.wait_ge(S.act.h, dsb_cap)
                for tt2 in range(2):
                    for osb in range(2):
                        gidx = 2 * tt2 + osb
                        v.wait_ge(S.pe.h, pe_q + gidx + 1)
                        col = 1024 * oh + 512 * osb
                        S.dve.inc(v.tensor_tensor(out=s_y2[tt2][:, col:col + 512],
                                                  in0=pgu2[gidx][:],
                                                  in1=s_dsb[:, col:col + 512], op=ALU.mult))
                        evac_caps[(tt2, osb)] = S.dve.n
            vec(f2dve)
            rel_d8 = S.pe.n
        rel_fq = S.pe.n

        for tt2 in range(2):
            gt = gts[tt2]
            y2_done = evac_caps[(tt2, 1)]
            ab4 = S.act.n
            db4 = S.dve.n

            def ldx1(sy, gt=gt, tt2=tt2, relx=rel_x1r[tt2], q=q):
                if q == 0 and tt2 == 0:
                    sy.wait_ge(S.dx1s.h, S.dx1s.n)
                if relx:
                    sy.wait_ge(S.dve.h, relx)
                S.dx1l.dma(sy.dma_start(out=s_x1r[tt2][:], in_=d_x1[gt]))
            sync(ldx1)
            x1_cap = S.dx1l.n

            def ep_act(s_, tt2=tt2, y2_done=y2_done, db4=db4):
                s_.wait_ge(S.dve.h, y2_done)
                s_.wait_ge(S.dve.h, db4)
                S.act.inc(s_.activation(out=s_sq4[:], in_=s_y2[tt2][:], func=ACTF.Square))
                s_.wait_ge(S.dve.h, db4 + 5)
                S.act.inc(s_.activation(out=s_st4[:, 6:7], in_=s_st4[:, 5:6], func=ACTF.Sqrt,
                                        bias=s_cb[:, CB_EPSLN:CB_EPSLN + 1]))
            sca(ep_act)

            def ep_dve(v, tt2=tt2, y2_done=y2_done, ab4=ab4, x1_cap=x1_cap,
                       relyo=rel_yo[tt2]):
                v.wait_ge(S.dve.h, y2_done)
                S.dve.inc(v.reduce_sum(out=s_st4[:, 0:1], in_=s_y2[tt2][:], axis=AX.X))
                v.wait_ge(S.act.h, ab4 + 1)
                v.reduce_sum(out=s_st4[:, 1:2], in_=s_sq4[:], axis=AX.X)
                S.dve.inc(v.tensor_scalar(out=s_st4[:, 2:3], in0=s_st4[:, 0:1],
                                          scalar1=1.0 / D, scalar2=None, op0=ALU.mult))
                S.dve.inc(v.tensor_scalar(out=s_st4[:, 3:4], in0=s_st4[:, 1:2],
                                          scalar1=1.0 / D, scalar2=None, op0=ALU.mult))
                S.dve.inc(v.tensor_scalar(out=s_st4[:, 4:5], in0=s_st4[:, 2:3],
                                          scalar1=s_st4[:, 2:3], scalar2=-1.0,
                                          op0=ALU.mult, op1=ALU.mult))
                v.tensor_copy(out=s_st4[:, 1:2], in_=s_st4[:, 3:4])
                S.dve.inc(v.tensor_tensor(out=s_st4[:, 5:6], in0=s_st4[:, 3:4],
                                          in1=s_st4[:, 4:5], op=ALU.add))
                v.wait_ge(S.act.h, ab4 + 2)
                S.dve.inc(v.reciprocal(out=s_st4[:, 7:8], in_=s_st4[:, 6:7]))
                v.tensor_copy(out=s_st4[:, 1:2], in_=s_st4[:, 5:6])
                S.dve.inc(v.tensor_scalar(out=s_t4[:], in0=s_y2[tt2][:],
                                          scalar1=s_st4[:, 2:3], scalar2=s_st4[:, 7:8],
                                          op0=ALU.subtract, op1=ALU.mult))
                v.wait_ge(S.dx1l.h, x1_cap)
                if relyo:
                    v.wait_ge(S.dyo.h, relyo)
                S.dve.inc(v.tensor_tensor(out=s_yo[tt2][:], in0=s_t4[:], in1=s_x1r[tt2][:],
                                          op=ALU.add))
            vec(ep_dve)
            yo_done = S.dve.n
            rel_x1r[tt2] = yo_done

            def ep_out(sy, gt=gt, tt2=tt2, yo_done=yo_done):
                sy.wait_ge(S.dve.h, yo_done)
                S.dyo.dma(sy.dma_start(out=d_y[128 * gt:128 * (gt + 1), :],
                                       in_=s_yo[tt2][:]))
            sync(ep_out)
            rel_yo[tt2] = S.dyo.n

    yo_total = S.dyo.n

    emit_dbgdump([(S.dyo.h, yo_total)])

    def fin(sy, yo_total=yo_total):
        sy.wait_ge(S.dyo.h, yo_total)
    sync(fin)

    p2b.__exit__(None, None, None)
    st.__exit__(None, None, None)
    return nc


_NC_CACHE = {}


def _get_nc():
    if "nc" not in _NC_CACHE:
        _NC_CACHE["nc"] = build_nc()
    return _NC_CACHE["nc"]


def kernel(L0_attn_x, L0_attn_wq, L0_attn_wk, L0_attn_wv, L0_attn_wo,
           L0_attn_q_norm_w, L0_attn_k_norm_w, L0_attn_rope_cos, L0_attn_rope_sin,
           L0_ffn_gate_w, L0_ffn_up_w, L0_ffn_down_w,
           L0_ffn_gate_scale, L0_ffn_up_scale, L0_ffn_down_scale):
    x = np.ascontiguousarray(np.asarray(L0_attn_x, dtype=np.float32))
    wq = np.ascontiguousarray(np.asarray(L0_attn_wq, dtype=np.float32))
    wk = np.ascontiguousarray(np.asarray(L0_attn_wk, dtype=np.float32))
    wv = np.ascontiguousarray(np.asarray(L0_attn_wv, dtype=np.float32))
    wo = np.ascontiguousarray(np.asarray(L0_attn_wo, dtype=np.float32))
    qw = np.asarray(L0_attn_q_norm_w, dtype=np.float32)
    kw = np.asarray(L0_attn_k_norm_w, dtype=np.float32)
    cos = np.asarray(L0_attn_rope_cos, dtype=np.float32)
    sin = np.asarray(L0_attn_rope_sin, dtype=np.float32)
    gw = np.ascontiguousarray(np.asarray(L0_ffn_gate_w, dtype=np.float32))
    uw = np.ascontiguousarray(np.asarray(L0_ffn_up_w, dtype=np.float32))
    dw = np.ascontiguousarray(np.asarray(L0_ffn_down_w, dtype=np.float32))
    gsc = np.asarray(L0_ffn_gate_scale, dtype=np.float32)
    usc = np.asarray(L0_ffn_up_scale, dtype=np.float32)
    dsc = np.asarray(L0_ffn_down_scale, dtype=np.float32)

    import ml_dtypes
    ident = np.eye(128, dtype=np.float32)
    identb = np.eye(128).astype(ml_dtypes.bfloat16)
    mask = np.where(np.arange(896)[None, :] - 384 >= np.arange(128)[:, None],
                    np.float32(0.0), np.float32(-1e9)).astype(ml_dtypes.bfloat16)
    cb = np.zeros((128, NCB), dtype=np.float32)
    cb[:, CB_EXPS] = 50.0
    cb[:, CB_EXPB] = -16.0
    cb[:, CB_EPSLN] = EPS_LN
    cb[:, CB_RKB] = 1e-6 * 2500.0 * 128.0
    cb[:, CB_RQB] = 1e-6
    cb[:, CB_RQS] = 1.0 / 128.0
    cb[:, CB_INVD] = 1.0 / D
    cb[:, CB_RKS] = 2500.0
    cb[:, CB_ONE] = 1.0
    sel2 = np.zeros((2, 128), dtype=np.float32)
    sel2[0, 0:64] = 1.0
    sel2[1, 64:128] = 1.0
    ones1b = np.ones((128, 1)).astype(ml_dtypes.bfloat16)
    onesr = np.ones((1, 128), dtype=np.float32)
    ones2 = np.zeros((128, 128), dtype=np.float32)
    ones2[0:64, 0] = 1.0
    ones2[64:128, 1] = 1.0
    gsct = np.ascontiguousarray(gsc.reshape(IC, 128).T)
    usct = np.ascontiguousarray(usc.reshape(IC, 128).T)
    dscr = np.ascontiguousarray(dsc.reshape(1, D))
    qwp = np.stack([np.concatenate([qw[0::2], qw[0::2]]),
                    np.concatenate([qw[1::2], qw[1::2]])], axis=1).astype(np.float32)
    kwp = np.stack([np.concatenate([kw[0::2], kw[0::2]]),
                    np.concatenate([kw[1::2], kw[1::2]])], axis=1).astype(np.float32)

    nc = _get_nc()
    in_maps = []
    for c in range(8):
        b, hp = divmod(c, 4)
        heads = [4 * hp + j for j in range(4)]
        rows = np.concatenate([np.arange(1024 * k + 256 * hp, 1024 * k + 256 * (hp + 1))
                               for k in range(4)])
        ropec = np.stack([np.concatenate([cos[heads[0]], cos[heads[1]]]),
                          np.concatenate([cos[heads[2]], cos[heads[3]]])], axis=1)
        ropes = np.stack([np.concatenate([sin[heads[0]], sin[heads[1]]]),
                          np.concatenate([sin[heads[2]], sin[heads[3]]])], axis=1)
        im = {
            "x_b": x[b],
            "x_res": np.ascontiguousarray(x[b][rows]),
            "wq_h": wq[512 * hp:512 * (hp + 1)],
            "wk_h": wk[512 * hp:512 * (hp + 1)],
            "wv_h": wv[512 * hp:512 * (hp + 1)],
            "wo_h": np.ascontiguousarray(wo[:, 512 * hp:512 * (hp + 1)]),
            "ropec": np.ascontiguousarray(ropec.astype(np.float32)),
            "ropes": np.ascontiguousarray(ropes.astype(np.float32)),
            "qwp": qwp, "kwp": kwp,
            "gate_w": gw, "up_w": uw, "down_w": dw,
            "gsct": gsct, "usct": usct, "dscr": dscr,
            "id32": ident, "idb": identb, "mask": mask, "cb": cb,
            "sel2": sel2, "ones1b": ones1b, "onesr": onesr, "ones2": ones2,
        }
        in_maps.append(im)

    res = run_bass_kernel_spmd(nc, in_maps, core_ids=list(range(8)))
    out = np.empty((B, T, D), dtype=np.float32)
    for c in range(8):
        b, hp = divmod(c, 4)
        y = res.results[c]["y"]
        for k in range(4):
            out[b, 1024 * k + 256 * hp:1024 * k + 256 * (hp + 1)] = y[256 * k:256 * (k + 1)]
    return out
